# version 22
# baseline (speedup 1.0000x reference)
import numpy as np
from contextlib import ExitStack

import concourse.bass as bass
import concourse.mybir as mybir
from concourse.bass_utils import run_bass_kernel_spmd

F32 = mybir.dt.float32
BF16 = mybir.dt.bfloat16
AF = mybir.ActivationFunctionType
ALU = mybir.AluOpType

ENGINES = ("sync", "scalar", "vector", "tensor", "gpsimd")


def real_cfg():
    return dict(n_cores=8, D=1024, HID=4096, H=16, DK=64, T=2048, TC=1024,
                slab_budget=4096)


def slab_geom(O, I, budget):
    KCH = I // 128
    G = min(O, max(128, (budget // KCH) // 128 * 128))
    while O % G:
        G -= 128
    return G, O // G, KCH * G


def build_program(cfg):
    D, HID, H, DK = cfg["D"], cfg["HID"], cfg["H"], cfg["DK"]
    T, TC = cfg["T"], cfg["TC"]
    CH = T // TC
    TT = TC // 128
    KD = D // 128
    KH = HID // 128
    KF = (H * DK) // 128
    NTOK = min(512, TC)
    NH = TC // NTOK
    WN = min(512, D)
    NW = D // WN
    SBGT = cfg["slab_budget"]
    HDK = H * DK
    assert HDK == D and KF == KD and H == 2 * KF

    wg = {}
    for name, O, I in [("wq", D, D), ("wk", D, D), ("wv", D, D), ("wo", D, D),
                       ("w1", HID, D), ("w2", HID, D),
                       ("h1", HID, HID), ("h2", HID, HID)]:
        G, NS, SF = slab_geom(O, I, SBGT)
        wg[name] = dict(O=O, I=I, G=G, NS=NS, SF=SF, MPS=G // 128, KS=I // 128)
    G3_ = min(256, D)
    KS3 = max(1, (KH * G3_ + SBGT - 1) // SBGT)
    while KH % KS3:
        KS3 += 1
    wg["w3"] = dict(O=D, I=HID, G=G3_, NS=(D // G3_) * KS3, SF=(KH // KS3) * G3_,
                    MPS=G3_ // 128, KS=KH // KS3)
    SLOT_F = max(w["SF"] for w in wg.values())
    NSLOT = 6

    nc = bass.Bass("TRN2", target_bir_lowering=False)

    x_d = nc.dram_tensor("x", [T, D], F32, kind="ExternalInput").ap()
    out_d = nc.dram_tensor("out", [T, D], F32, kind="ExternalOutput").ap()
    w_d = {k: nc.dram_tensor("w_" + k, [wg[k]["NS"], 128, wg[k]["SF"]], F32,
                             kind="ExternalInput").ap() for k in wg}
    scl_d = nc.dram_tensor("scl", [128, 4 * KH], F32, kind="ExternalInput").ap()
    s3b_d = nc.dram_tensor("s3b", [128, D], F32, kind="ExternalInput").ap()
    ident_d = nc.dram_tensor("ident", [128, 128], BF16, kind="ExternalInput").ap()
    cc_in = nc.dram_tensor("cc_in", [64, HDK], F32)
    cc_out = nc.dram_tensor("cc_out", [64, HDK], F32)

    est = ExitStack()
    sbt = lambda n, s, d: est.enter_context(nc.sbuf_tensor(n, s, d))
    pst = lambda n, s, d: est.enter_context(nc.psum_tensor(n, s, d))

    xnf = sbt("xnf", [128, KD * TC], BF16)
    ring = sbt("ring", [128, NSLOT * SLOT_F], BF16)
    hA = sbt("hA", [128, KH * TC], BF16)
    hB = sbt("hB", [128, KH * TC], BF16)
    scratch = sbt("scratch", [128, max(TC, 2 * WN)], F32)
    s3b = sbt("sb_s3b", [128, D], F32)
    scl = sbt("sb_scl", [128, 4 * KH], F32)
    kvacc = sbt("kvacc", [64, HDK], F32)
    kvbf = sbt("kvbf", [128, HDK], BF16)
    ident = sbt("sb_ident", [128, 128], BF16)
    NRAW = CH * TT * 2
    RSTD0, NMR0, ST60 = NRAW, NRAW + CH * TT, NRAW + 2 * CH * TT
    stats = sbt("stats", [128, ST60 + 24], F32)
    hAf = hA.bitcast(F32)
    hBf = hB.bitcast(F32)

    XNT1, KT0, VT0 = 0, TT * D, 2 * TT * D
    XNT2 = 3 * TT * D
    QF0, AF0 = 2 * TT * D, 3 * TT * D
    assert KH * TC >= 4 * TT * D

    psmm = [pst("psA", [128, 512], F32), pst("psB", [128, 512], F32),
            pst("psC", [128, 512], F32), pst("psD", [128, 512], F32)]
    pskv = pst("pskv", [128, 1024], F32)
    pstr = pst("pstr", [128, 2048], BF16)
    pstrf = pstr.bitcast(F32)
    PS8 = [psmm[0][:, :], psmm[1][:, :], psmm[2][:, :], psmm[3][:, :],
           pskv[:, 0:512], pskv[:, 512:1024], pstrf[:, 0:512], pstrf[:, 512:1024]]

    sems = {n: est.enter_context(nc.semaphore(n))
            for n in (("pe", "act", "dve", "po", "cc", "qc", "qd", "qi", "qx", "qo")
                      + tuple("qg%d" % i for i in range(NSLOT)))}
    cnt = {n: 0 for n in sems}
    steps = {e: [] for e in ENGINES}
    hwm = {}

    def wait(eng, s, v):
        if v is None or v <= 0 or hwm.get((eng, s), 0) >= v:
            return
        hwm[(eng, s)] = v
        import sys as _sys
        steps[eng].append(("w", s, v, _sys._getframe(1).f_lineno))

    def emit(eng, fn, inc=None, amt=1):
        tok = None
        if inc is not None:
            cnt[inc] += amt
            tok = cnt[inc]
        import sys as _sys
        steps[eng].append(("i", fn, inc, amt, _sys._getframe(1).f_lineno))
        return tok

    def dma(eng, qsem, dst, src):
        if qsem not in ("qc",):
            wait(eng, qsem, cnt[qsem])
        return emit(eng, lambda e, o=dst, i=src: e.dma_start(out=o, in_=i), qsem, 16)

    def mm(pp, lhs, rhs, start, stop, inc=False):
        return emit("tensor",
                    lambda e, o=pp, l=lhs, r=rhs, st=start, sp=stop:
                    e.matmul(o, l, r, start=st, stop=sp),
                    "pe" if (inc and stop) else None, 1)

    def tr(pp, src):
        return emit("tensor", lambda e, o=pp, i=src: e.transpose(o, i, ident[:, :]),
                    "pe", 1)

    def act(out, in_, func, scale=None, bias=None):
        def f(e, o=out, i=in_, fu=func, sc=scale, bi=bias):
            kw = {}
            if sc is not None:
                kw["scale"] = sc
            if bi is not None:
                kw["bias"] = bi
            return e.activation(o, i, fu, **kw)
        return emit("scalar", f, "act", 1)

    t_ident_qi = dma("sync", "qd", ident[:, :], ident_d[:, :])

    def load_constants():
        dma("sync", "qc", scl[:, :], scl_d[:, :])
        dma("sync", "qc", s3b[:, :], s3b_d[:, :])
        return cnt["qc"]
    t_scl = t_s3b = 32

    slot_free = [0] * NSLOT
    slab_gi = [0]

    def load_slab(name, s):
        slot = slab_gi[0] % NSLOT
        slab_gi[0] += 1
        wait("gpsimd", "pe", slot_free[slot])
        tok_ = dma("gpsimd", "qg%d" % slot,
                   ring[:, slot * SLOT_F: slot * SLOT_F + wg[name]["SF"]],
                   w_d[name][s])
        return ("qg%d" % slot, tok_), slot

    def stage_sync():
        wait("tensor", "act", cnt["act"])
        wait("tensor", "dve", cnt["dve"])
        wait("tensor", "po", cnt["po"])

    def wtile(name, slot, k, col, width):
        base = slot * SLOT_F + k * wg[name]["G"] + col
        return ring[:, base:base + width]

    tok = dict(pe_kv=0, dve_kvevac=0, act_norm={}, dve_bn={}, pe_tr={},
               pe_h1={}, pe_h2={}, pe_w3={}, pe_wo={}, dve_comb={},
               pe_attn={}, act_q={}, qx_x0h=0,
               act_kvready=0, act_trevac=0, qs_kvdup=0, qo_out=0)

    XNT = lambda ch: (0 if ch % 2 == 0 else 3 * TT * D)
    KT0, VT0 = TT * D, 2 * TT * D
    QF0 = 2 * TT * D
    AFH = 2 * TT * D
    XB = lambda ch: (ch % 2) * TT * D

    norm_toks = {}

    def stats_and_norms(ch):
        wait("sync", "act", tok["act_norm"].get(ch - 2, 0))
        wait("sync", "dve", tok["dve_bn"].get(ch - 2, 0))
        NXS = max(1, TT // 2) if ch else max(1, TT // 1)
        t_xs = {}
        for xs in range(NXS):
            r0 = ch * TC + xs * (TC // NXS)
            srcx = x_d[r0:r0 + TC // NXS, :].rearrange("(t p) f -> p t f", p=128)
            o0 = XB(ch) + xs * (TT // NXS) * D
            dstx = hAf[:, o0:o0 + (TT // NXS) * D].rearrange("p (t f) -> p t f", f=D)
            t_xs[xs] = dma("sync", "qx", dstx, srcx)
        if ch == 0:
            tok["qx_x0h"] = t_xs[max(0, NXS // 2 - 1)]
            load_constants()

        cols = lambda t: (stats[:, 2 * (ch * TT + t):2 * (ch * TT + t) + 1],
                          stats[:, 2 * (ch * TT + t) + 1:2 * (ch * TT + t) + 2],
                          stats[:, RSTD0 + ch * TT + t:RSTD0 + ch * TT + t + 1],
                          stats[:, NMR0 + ch * TT + t:NMR0 + ch * TT + t + 1])
        eps_tok = {}
        for t in range(TT):
            wait("vector", "qx", t_xs[t // (TT // NXS)])
            xt = hAf[:, XB(ch) + t * D:XB(ch) + (t + 1) * D]
            st6 = stats[:, ST60 + (t % 2) * 12:ST60 + (t % 2) * 12 + 6]
            st6b = stats[:, ST60 + (t % 2) * 12 + 6:ST60 + (t % 2) * 12 + 12]
            emit("vector", lambda e, o=st6, i=xt[:, 0:D // 2]: e.bn_stats(o, i),
                 "dve", 1)
            b1 = emit("vector", lambda e, o=st6b, i=xt[:, D // 2:D]: e.bn_stats(o, i),
                      "dve", 1)
            wait("vector", "dve", b1)
            mean_c, var_c, rstd_c, nmr_c = cols(t)
            ag = emit("vector", lambda e, o=stats[:, 2 * (ch * TT + t):2 * (ch * TT + t) + 2],
                      i=stats[:, ST60 + (t % 2) * 12:ST60 + (t % 2) * 12 + 12]:
                      e.bn_aggr(o, i), "dve", 1)
            wait("vector", "dve", ag)
            eps_tok[t] = emit("vector", lambda e, o=var_c, i=var_c:
                              e.tensor_scalar_add(o, i, 1e-5), "dve", 1)
        ex_tok = {}
        for t in range(TT):
            mean_c, var_c, rstd_c, nmr_c = cols(t)
            wait("scalar", "dve", eps_tok[t])
            ln = act(var_c, var_c, AF.Ln)
            wait("scalar", "act", ln)
            ex_tok[t] = act(rstd_c, var_c, AF.Exp, scale=-0.5)
        nm_tok = {}
        for t in range(TT):
            mean_c, var_c, rstd_c, nmr_c = cols(t)
            wait("vector", "act", ex_tok[t])
            nm = emit("vector", lambda e, o=nmr_c, a=mean_c, b=rstd_c:
                      e.tensor_mul(o, a, b), "dve", 1)
            wait("vector", "dve", nm)
            nm_tok[t] = emit("vector", lambda e, o=nmr_c, i=nmr_c:
                             e.tensor_scalar_mul(o, i, -1.0), "dve", 1)
        tok["dve_bn"][ch] = nm_tok[TT - 1]
        for t in range(TT):
            mean_c, var_c, rstd_c, nmr_c = cols(t)
            wait("scalar", "dve", nm_tok[t])
            wait("scalar", "pe", tok["pe_kv"])
            norm_toks[(ch, t)] = act(
                hB[:, XNT(ch) + t * D:XNT(ch) + (t + 1) * D],
                hAf[:, XB(ch) + t * D:XB(ch) + (t + 1) * D], AF.Identity,
                scale=rstd_c, bias=nmr_c)
        tok["act_norm"][ch] = norm_toks[(ch, TT - 1)]

    def do_transposes(ch, src_off, src_buf=None, ntoks=None):
        buf = hB if src_buf is None else src_buf
        wait("tensor", "dve", cnt["dve"])
        evac = {}
        xnf_r = xnf.rearrange("p (j tc) -> p j tc", tc=TC)
        last_tr = None
        for t in range(TT):
            half = t % 2
            nt = norm_toks.get((ch, t), 0) if ntoks is None else ntoks.get(t, 0)
            wait("tensor", "act", nt or 0)
            wait("tensor", "qd", t_ident_qi)
            wait("tensor", "act", tok["act_trevac"])
            wait("tensor", "act", evac.get(t - 2, 0))
            for j in range(KD):
                last_tr = tr(pstr[:, half * 1024 + j * 128:half * 1024 + (j + 1) * 128],
                             buf[:, src_off + t * D + j * 128:src_off + t * D + (j + 1) * 128])
            wait("scalar", "pe", last_tr)
            src_t = pstr[:, half * 1024:half * 1024 + KD * 128].rearrange(
                "p (j c) -> p j c", c=128)
            evac[t] = act(xnf_r[:, :, t * 128:(t + 1) * 128], src_t, AF.Copy)
        tok["act_trevac"] = evac[TT - 1]
        return last_tr, evac[TT - 1], evac

    p1b0_kv0 = [0]

    def phase1b(ch, last_tr, xnf_ready, tr_evacs):
        tok["pe_tr"][ch] = last_tr

        stage_sync()
        if ch == 0:
            wait("gpsimd", "qx", tok.get("qx_x0h", 0))
        else:
            wait("gpsimd", "pe", p1b0_kv0[0])
        GW = wg["wk"]["G"]
        kslabs = [load_slab("wk", s) for s in range(wg["wk"]["NS"])]
        vslabs = [load_slab("wv", s) for s in range(wg["wv"]["NS"])]
        kv_evac = {}
        for t in range(TT):
            for wt, _ in kslabs + vslabs:
                wait("tensor", wt[0], wt[1])
            wait("tensor", "act", tr_evacs[t])
            q8 = (t % 2) * 4
            wait("tensor", "dve", kv_evac.get((t - 2, NW - 1, 1), 0))
            for wi, (wname, slabs) in enumerate((("wk", kslabs), ("wv", vslabs))):
                g = None
                for ow in range(NW):
                    s_i = (ow * WN) // GW
                    col = (ow * WN) % GW
                    wsl = slabs[s_i][1]
                    for k in range(KD):
                        g = mm(PS8[q8 + 2 * wi + ow][:, 0:WN],
                               xnf[:, k * TC + t * 128:k * TC + (t + 1) * 128],
                               wtile(wname, wsl, k, col, WN),
                               k == 0, k == KD - 1, inc=(k == KD - 1 and ow == NW - 1))
                wait("vector", "pe", g)
                dst0 = KT0 if wi == 0 else VT0
                for ow in range(NW):
                    kv_evac[(t, ow, wi)] = emit(
                        "vector",
                        lambda e, o=hB[:, dst0 + t * D + ow * WN:
                                       dst0 + t * D + ow * WN + WN],
                        i=PS8[q8 + 2 * wi + ow][:, 0:WN]: e.tensor_copy(o, i), "dve", 1)
            if ch == 0 and t == 0:
                p1b0_kv0[0] = cnt["pe"]
        for _, wsl in kslabs + vslabs:
            slot_free[wsl] = cnt["pe"]
        kv_proj_done_act = cnt["dve"]

        stage_sync()
        last_kv = None
        for h in range(H):
            wait("tensor", "dve", tok["dve_kvevac"])
            wait("tensor", "dve", kv_proj_done_act)
            for t in range(TT):
                last_kv = mm(pskv[0:64, h * DK:(h + 1) * DK],
                             hB[:, KT0 + t * D + h * DK:KT0 + t * D + (h + 1) * DK],
                             hB[:, VT0 + t * D + h * DK:VT0 + t * D + (h + 1) * DK],
                             t == 0, t == TT - 1, inc=(t == TT - 1))
        tok["pe_kv"] = last_kv
        wait("vector", "pe", last_kv)
        if ch == 0:
            tok["dve_kvevac"] = emit("vector", lambda e, o=kvacc[:, :],
                                     i=pskv[0:64, 0:HDK]: e.tensor_copy(o, i),
                                     "dve", 1)
        else:
            tok["dve_kvevac"] = emit("vector", lambda e, o=kvacc[:, :],
                                     a=kvacc[:, :], b=pskv[0:64, 0:HDK]:
                                     e.tensor_add(o, a, b), "dve", 1)


    stats_and_norms(0)
    tr0 = do_transposes(0, XNT(0))
    if CH > 1:
        stats_and_norms(1)
    phase1b(0, *tr0)
    for ch in range(1, CH):
        if ch + 1 < CH:
            stats_and_norms(ch + 1)
        trc = do_transposes(ch, XNT(ch))
        phase1b(ch, *trc)

    wait("gpsimd", "dve", tok["dve_kvevac"])
    t_ccin = dma("gpsimd", "qi", cc_in.ap(), kvacc[:, :])
    wait("gpsimd", "qi", t_ccin)
    n_cores = cfg["n_cores"]
    emit("gpsimd", lambda e: e.collective_compute(
        "AllReduce", ALU.add, replica_groups=[list(range(n_cores))],
        ins=[cc_in.ap().opt()], outs=[cc_out.ap().opt()]), "cc", 1)
    t_cc = cnt["cc"]

    prep = {}
    for ch in range(CH):
        if ch == 0:
            last_tr, xnf_ready, _ = do_transposes(ch, XNT(ch))
        else:
            last_tr, xnf_ready = prep[ch]
        tok["pe_tr"][ch] = last_tr

        stage_sync()
        pre_pair = tok.pop("pre_w1w2", None) if ch in tok.get("pre_for", {}) else None
        if tok.get("pre_for_ch") == ch:
            pre_pair = tok.pop("pre_w1w2")
        else:
            pre_pair = None
        NSw = wg["w1"]["NS"]
        MPS = wg["w1"]["MPS"]
        a_evac = {}
        h_done = {}
        dve_bn_last = tok["dve_bn"].get(CH - 1, 0)

        def w1w2_compute(s, w1t, w1s, w2t, w2s):
            for mi in range(MPS):
                m = s * MPS + mi
                q8 = (m % 2) * 4
                wait("tensor", w1t[0], w1t[1])
                wait("tensor", "act", xnf_ready)
                wait("tensor", "act", a_evac.get((m - 2, NH - 1), 0))
                wait("tensor", "dve", h_done.get((m - 2, NH - 1), 0))
                g1 = {}
                for nh in range(NH):
                    for k in range(KD):
                        g1[nh] = mm(PS8[q8 + nh][:, 0:NTOK],
                                    wtile("w1", w1s, k, mi * 128, 128),
                                    xnf[:, k * TC + nh * NTOK:k * TC + (nh + 1) * NTOK],
                                    k == 0, k == KD - 1,
                                    inc=(k == KD - 1))
                wait("tensor", w2t[0], w2t[1])
                g2 = {}
                for nh in range(NH):
                    for k in range(KD):
                        g2[nh] = mm(PS8[q8 + 2 + nh][:, 0:NTOK],
                                    wtile("w2", w2s, k, mi * 128, 128),
                                    xnf[:, k * TC + nh * NTOK:k * TC + (nh + 1) * NTOK],
                                    k == 0, k == KD - 1,
                                    inc=(k == KD - 1))
                for nh in range(NH):
                    wait("scalar", "pe", g1[nh])
                    wait("scalar", "qc", t_scl)
                    wait("scalar", "dve", h_done.get((m - 1, nh), 0))
                    a_evac[(m, nh)] = act(scratch[:, nh * NTOK:(nh + 1) * NTOK],
                                          PS8[q8 + nh][:, 0:NTOK], AF.Silu,
                                          scale=scl[:, m:m + 1])
                    wait("vector", "pe", g2[nh])
                    wait("vector", "act", a_evac[(m, nh)])
                    wait("vector", "qc", t_scl)
                    wait("vector", "dve", dve_bn_last)
                    wait("vector", "qo", tok["qo_out"])
                    h_done[(m, nh)] = emit(
                        "vector",
                        lambda e, o=hA[:, m * TC + nh * NTOK:m * TC + (nh + 1) * NTOK],
                        i0=PS8[q8 + 2 + nh][:, 0:NTOK],
                        sc=scl[:, KH + m:KH + m + 1],
                        i1=scratch[:, nh * NTOK:(nh + 1) * NTOK]:
                        e.scalar_tensor_tensor(o, i0, sc, i1, ALU.mult, ALU.mult),
                        "dve", 1)
            return cnt["pe"]

        prev = None
        for s in range(NSw):
            if prev is not None:
                done = w1w2_compute(*prev)
                slot_free[prev[2]] = done
                slot_free[prev[4]] = done
            if s == 0 and pre_pair is not None:
                (w1t, w1s), (w2t, w2s) = pre_pair
            else:
                w1t, w1s = load_slab("w1", s)
                w2t, w2s = load_slab("w2", s)
            prev = (s, w1t, w1s, w2t, w2s)
        done = w1w2_compute(*prev)
        slot_free[prev[2]] = done
        slot_free[prev[4]] = done
        h_all_dve = cnt["dve"]

        def hid_layer(wname, src_buf, dst_buf, scl_off, pe_tok_dict,
                      src_ready_eng, src_ready_tok):
            NSh = wg[wname]["NS"]
            MPSh = wg[wname]["MPS"]
            ev = {}
            prevh = None

            def comp(s, wt, wsl):
                for mi in range(MPSh):
                    m = s * MPSh + mi
                    nq = 8 // NH
                    q8 = (m % nq) * NH
                    wait("tensor", wt[0], wt[1])
                    wait("tensor", src_ready_eng, src_ready_tok)
                    dist = nq
                    wait("tensor", "act", ev.get((m - dist, NH - 1), 0))
                    g = {}
                    for nh in range(NH):
                        for k in range(KH):
                            g[nh] = mm(PS8[q8 + nh][:, 0:NTOK],
                                       wtile(wname, wsl, k, mi * 128, 128),
                                       src_buf[:, k * TC + nh * NTOK:
                                               k * TC + (nh + 1) * NTOK],
                                       k == 0, k == KH - 1, inc=(k == KH - 1))
                    for nh in range(NH):
                        wait("scalar", "pe", g[nh])
                        wait("scalar", "qc", t_scl)
                        ev[(m, nh)] = act(
                            dst_buf[:, m * TC + nh * NTOK:m * TC + (nh + 1) * NTOK],
                            PS8[q8 + nh][:, 0:NTOK], AF.Gelu,
                            scale=scl[:, scl_off + m:scl_off + m + 1])
                return cnt["pe"]

            stage_sync()
            for s in range(NSh):
                if prevh is not None:
                    d = comp(*prevh)
                    slot_free[prevh[2]] = d
                wt, wsl = load_slab(wname, s)
                prevh = (s, wt, wsl)
            d = comp(*prevh)
            slot_free[prevh[2]] = d
            pe_tok_dict[ch] = cnt["pe"]
            return cnt["act"]

        h1_act_done = hid_layer("h1", hA, hB, 2 * KH, tok["pe_h1"],
                                "dve", h_all_dve)
        h2_act_done = hid_layer("h2", hB, hA, 3 * KH, tok["pe_h2"],
                                "act", h1_act_done)

        G3 = wg["w3"]["G"]
        KS3c = KH // wg["w3"]["KS"]
        KSL3 = wg["w3"]["KS"]
        NOG3 = D // G3
        g_ev = {}
        g_pm = {}
        w3_pm_last = [0]

        def w3_comp(og, slabs):
            for t in range(TT):
                gi = og * TT + t
                pp = PS8[gi % 8]
                for wt, _ in slabs:
                    wait("tensor", wt[0], wt[1])
                wait("tensor", "act", h2_act_done)
                wait("tensor", "dve", g_pm.get(gi - 8, 0))
                g = None
                for ks in range(KS3c):
                    wsl = slabs[ks][1]
                    for kl in range(KSL3):
                        k = ks * KSL3 + kl
                        g = mm(pp[:, 0:G3],
                               hA[:, k * TC + t * 128:k * TC + (t + 1) * 128],
                               wtile("w3", wsl, kl, 0, G3),
                               k == 0, k == KH - 1, inc=(k == KH - 1))
                wait("vector", "pe", g)
                wait("vector", "qc", t_s3b)
                wait("vector", "dve", h_all_dve)
                wait("vector", "act", g_ev.get(gi - 2, 0))
                sl = (gi % 2) * G3
                g_pm[gi] = emit("vector",
                                lambda e, o=scratch[:, sl:sl + G3], a=pp[:, 0:G3],
                                b=s3b[:, og * G3:(og + 1) * G3]: e.tensor_mul(o, a, b),
                                "dve", 1)
                w3_pm_last[0] = g_pm[gi]
                wait("scalar", "dve", g_pm[gi])
                wait("scalar", "pe", tok["pe_tr"][ch])
                wait("scalar", "pe", tok["pe_kv"])
                wait("scalar", "act", h2_act_done)
                g_ev[gi] = act(hBf[:, t * D + og * G3:t * D + (og + 1) * G3],
                               scratch[:, sl:sl + G3], AF.Silu)
            return cnt["pe"]

        stage_sync()
        prev3 = None
        for og in range(NOG3):
            if prev3 is not None:
                d = w3_comp(*prev3)
                for _, wsl in prev3[1]:
                    slot_free[wsl] = d
            slabs = [load_slab("w3", og * KS3c + ks) for ks in range(KS3c)]
            prev3 = (og, slabs)
        d = w3_comp(*prev3)
        for _, wsl in prev3[1]:
            slot_free[wsl] = d
        tok["pe_w3"][ch] = cnt["pe"]
        g_act_done = cnt["act"]

        if ch == 0:
            wait("sync", "cc", t_cc)
            t_ccout = dma("sync", "qx", kvacc[:, :], cc_out.ap())
            wait("scalar", "qx", t_ccout)
            tok["act_kvready"] = act(kvbf[0:64, :], kvacc[:, :], AF.Copy,
                                     scale=float(1.0 / np.sqrt(DK)))
            wait("sync", "act", tok["act_kvready"])
            tok["qs_kvdup"] = dma("sync", "qx", kvbf[64:128, :], kvbf[0:64, :])

        wait("sync", "pe", tok["pe_w3"][ch])
        wait("sync", "qo", tok["qo_out"])
        src = x_d[ch * TC:(ch + 1) * TC, :].rearrange("(t p) f -> p t f", p=128)
        dstx = hAf[:, 0:TT * D].rearrange("p (t f) -> p t f", f=D)
        t_xres = dma("sync", "qx", dstx, src)

        NGI3 = NOG3 * TT
        qslabs = [load_slab("wq", s) for s in range(wg["wq"]["NS"])]
        MPSQ = wg["wq"]["MPS"]
        q_ev = {}
        for m in range(KF):
            bank0 = (m % 2) * 2
            wqt, wqs = qslabs[m // MPSQ]
            wait("tensor", wqt[0], wqt[1])
            for nh in range(NH):
                b = bank0 + nh
                if NGI3 > b:
                    gi_last = NGI3 - 1 - ((NGI3 - 1 - b) % 8)
                    wait("tensor", "dve", g_pm.get(gi_last, 0))
                else:
                    wait("tensor", "act", h2_act_done)
            wait("tensor", "dve", q_ev.get((m - 2, NH - 1), 0))
            g = {}
            for nh in range(NH):
                for k in range(KD):
                    g[nh] = mm(psmm[bank0 + nh][:, 0:NTOK],
                               wtile("wq", wqs, k, (m % MPSQ) * 128, 128),
                               xnf[:, k * TC + nh * NTOK:k * TC + (nh + 1) * NTOK],
                               k == 0, k == KD - 1, inc=(k == KD - 1))
            for nh in range(NH):
                wait("vector", "pe", g[nh])
                wait("vector", "pe", tok["pe_kv"])
                wait("vector", "act", h2_act_done)
                wait("vector", "dve", w3_pm_last[0])
                q_ev[(m, nh)] = emit(
                    "vector",
                    lambda e, o=hB[:, QF0 + m * TC + nh * NTOK:
                                   QF0 + m * TC + (nh + 1) * NTOK],
                    i=psmm[bank0 + nh][:, 0:NTOK]: e.tensor_copy(o, i), "dve", 1)
        for _, wsl in qslabs:
            slot_free[wsl] = cnt["pe"]
        q_act_done = cnt["dve"]
        tok["act_q"][ch] = q_act_done

        at_ev = {}
        for kf in range(KF):
            bank0 = (kf % 2) * 2
            wait("tensor", "act", tok["act_kvready"])
            wait("tensor", "qx", tok["qs_kvdup"])
            wait("tensor", "dve", q_ev.get((kf, NH - 1), 0))
            wait("tensor", "dve", q_ev.get((KF - 2 + (kf % 2), NH - 1), 0))
            wait("tensor", "act", at_ev.get((kf - 2, NH - 1), 0))
            for par in range(2):
                h = 2 * kf + par
                for nh in range(NH):
                    mm(psmm[bank0 + nh][par * 64:(par + 1) * 64, 0:NTOK],
                       kvbf[par * 64:par * 64 + 64, h * DK:(h + 1) * DK],
                       hB[par * 64:(par + 1) * 64,
                          QF0 + kf * TC + nh * NTOK:QF0 + kf * TC + (nh + 1) * NTOK],
                       True, True, inc=(par == 1 and nh == NH - 1))
            gl = cnt["pe"]
            for nh in range(NH):
                wait("scalar", "pe", gl)
                wait("scalar", "pe", tok["pe_w3"][ch])
                at_ev[(kf, nh)] = act(hA[:, AFH + kf * TC + nh * NTOK:
                                         AFH + kf * TC + (nh + 1) * NTOK],
                                      psmm[bank0 + nh][:, 0:NTOK], AF.Copy)
        at_act_done = cnt["act"]
        tok["pe_attn"][ch] = cnt["pe"]
        stage_sync()

        if ch + 1 < CH:
            nch = ch + 1
            wait("sync", "pe", tok["pe_attn"][ch])
            wait("sync", "dve", tok["act_q"][ch])
            NXS2 = max(1, TT // 4)
            tx2 = {}
            for xs in range(NXS2):
                r0 = nch * TC + xs * (TC // NXS2)
                srcx = x_d[r0:r0 + TC // NXS2, :].rearrange("(t p) f -> p t f", p=128)
                o0 = TT * D + xs * (TT // NXS2) * D
                dstx2 = hBf[:, o0:o0 + (TT // NXS2) * D].rearrange(
                    "p (t f) -> p t f", f=D)
                tx2[xs] = dma("sync", "qx", dstx2, srcx)
            rn = {}
            for t in range(TT):
                wait("scalar", "qx", tx2[t // (TT // NXS2)])
                wait("scalar", "pe", tok["pe_w3"][ch])
                rn[t] = act(hA[:, 3 * TT * D + t * D:3 * TT * D + (t + 1) * D],
                            hBf[:, TT * D + t * D:TT * D + (t + 1) * D], AF.Identity,
                            scale=stats[:, RSTD0 + nch * TT + t:RSTD0 + nch * TT + t + 1],
                            bias=stats[:, NMR0 + nch * TT + t:NMR0 + nch * TT + t + 1])
            tok["prep_rn"] = rn

        oslabs = [load_slab("wo", s) for s in range(wg["wo"]["NS"])]
        if ch + 1 < CH:
            tok["pre_w1w2"] = (load_slab("w1", 0), load_slab("w2", 0))
            tok["pre_for_ch"] = ch + 1
        GWO = wg["wo"]["G"]
        comb_pm = {}
        comb = {}
        last_comb = w3_pm_last[0]
        for t in range(TT):
            bank0 = (t % 2) * 2
            for wt, _ in oslabs:
                wait("tensor", wt[0], wt[1])
            wait("tensor", "act", at_act_done)
            wait("tensor", "dve", comb_pm.get((t - 2, NW - 1), 0))
            g = {}
            for ow in range(NW):
                s_i = (ow * WN) // GWO
                col = (ow * WN) % GWO
                wos = oslabs[s_i][1]
                for kf in range(KF):
                    g[ow] = mm(psmm[bank0 + ow][:, 0:WN],
                               hA[:, AFH + kf * TC + t * 128:AFH + kf * TC + (t + 1) * 128],
                               wtile("wo", wos, kf, col, WN),
                               kf == 0, kf == KF - 1, inc=(kf == KF - 1))
            for ow in range(NW):
                wait("vector", "pe", g[ow])
                wait("vector", "act", g_act_done)
                wait("vector", "dve", w3_pm_last[0])
                wait("vector", "po", comb.get((t - 1, ow), 0))
                sl = (ow % 2) * WN
                comb_pm[(t, ow)] = emit(
                    "vector",
                    lambda e, o=scratch[:, sl:sl + WN], a=psmm[bank0 + ow][:, 0:WN],
                    b=hBf[:, t * D + ow * WN:t * D + ow * WN + WN]:
                    e.tensor_mul(o, a, b), "dve", 1)
                wait("gpsimd", "dve", comb_pm[(t, ow)])
                wait("gpsimd", "qx", t_xres)
                last_comb = emit(
                    "gpsimd",
                    lambda e, o=hAf[:, t * D + ow * WN:t * D + ow * WN + WN],
                    a=scratch[:, sl:sl + WN],
                    b=hAf[:, t * D + ow * WN:t * D + ow * WN + WN]:
                    e.tensor_add(o, a, b), "po", 1)
                comb[(t, ow)] = last_comb
            wait("sync", "po", last_comb)
            dsto = out_d[ch * TC + t * 128:ch * TC + (t + 1) * 128, :]
            tok["qo_out"] = dma("sync", "qo", dsto, hAf[:, t * D:(t + 1) * D])
            tok.setdefault("qo_tiles", {}).setdefault(ch, {})[t] = tok["qo_out"]
        for _, wsl in oslabs:
            slot_free[wsl] = cnt["pe"]
        tok["pe_wo"][ch] = cnt["pe"]
        tok["dve_comb"][ch] = cnt["po"]

        if ch + 1 < CH:
            p_tr, p_ready, _ = do_transposes(ch + 1, 3 * TT * D, src_buf=hA,
                                             ntoks=tok.pop("prep_rn"))
            prep[ch + 1] = (p_tr, p_ready)

    wait("sync", "qo", cnt["qo"])
    wait("sync", "qx", cnt["qx"])

    with nc.Block() as block:
        def body(engine_name):
            def run(e):
                for st in steps[engine_name]:
                    if st[0] == "w":
                        e.wait_ge(sems[st[1]], st[2])
                    else:
                        ins = st[1](e)
                        if st[2] is not None:
                            ins.then_inc(sems[st[2]], st[3])
            return run
        block.sync(body("sync"))
        block.scalar(body("scalar"))
        block.vector(body("vector"))
        block.tensor(body("tensor"))
        block.gpsimd(body("gpsimd"))

    sink = globals().get("DEBUG_SINK")
    if isinstance(sink, list):
        sink.append({e: list(steps[e]) for e in steps})
    est.close()
    return nc


def _slab_pack(W, budget, kslices=1):
    O, I = W.shape
    if kslices == 1:
        G, NS, SF = slab_geom(O, I, budget)
        return np.ascontiguousarray(
            W.reshape(NS, G, I // 128, 128).transpose(0, 3, 2, 1)).reshape(NS, 128, SF)
    G = min(256, O)
    NOG = O // G
    KCH = I // 128
    KSL = KCH // kslices
    A = W.reshape(NOG, G, kslices, KSL, 128).transpose(0, 2, 4, 3, 1)
    return np.ascontiguousarray(A).reshape(NOG * kslices, 128, KSL * G)


def kernel(L0_attn_x, L0_attn_W_q, L0_attn_W_k, L0_attn_W_v, L0_attn_W_o,
           L0_ffn_w1_w, L0_ffn_w1_s, L0_ffn_w2_w, L0_ffn_w2_s,
           L0_ffn_h1_w, L0_ffn_h1_s, L0_ffn_h2_w, L0_ffn_h2_s,
           L0_ffn_w3_w, L0_ffn_w3_s):
    cfg = real_cfg()
    NC, T, D, HID = cfg["n_cores"], cfg["T"], cfg["D"], cfg["HID"]
    KH = HID // 128
    B = cfg["slab_budget"]
    x = np.ascontiguousarray(np.asarray(L0_attn_x, dtype=np.float32).reshape(-1, D))
    S = x.shape[0]
    assert S == NC * T

    packs = {
        "w_wq": _slab_pack(np.asarray(L0_attn_W_q, np.float32), B),
        "w_wk": _slab_pack(np.asarray(L0_attn_W_k, np.float32), B),
        "w_wv": _slab_pack(np.asarray(L0_attn_W_v, np.float32), B),
        "w_wo": _slab_pack(np.asarray(L0_attn_W_o, np.float32), B),
        "w_w1": _slab_pack(np.asarray(L0_ffn_w1_w, np.float32), B),
        "w_w2": _slab_pack(np.asarray(L0_ffn_w2_w, np.float32), B),
        "w_h1": _slab_pack(np.asarray(L0_ffn_h1_w, np.float32), B),
        "w_h2": _slab_pack(np.asarray(L0_ffn_h2_w, np.float32), B),
        "w_w3": _slab_pack(np.asarray(L0_ffn_w3_w, np.float32), B,
                           kslices=max(1, (HID // 128 * min(256, D) + B - 1) // B)),
    }
    scl = np.empty((128, 4 * KH), np.float32)
    scl[:, 0 * KH:1 * KH] = np.asarray(L0_ffn_w1_s, np.float32).reshape(KH, 128).T
    scl[:, 1 * KH:2 * KH] = np.asarray(L0_ffn_w2_s, np.float32).reshape(KH, 128).T
    scl[:, 2 * KH:3 * KH] = np.asarray(L0_ffn_h1_s, np.float32).reshape(KH, 128).T
    scl[:, 3 * KH:4 * KH] = np.asarray(L0_ffn_h2_s, np.float32).reshape(KH, 128).T
    s3b = np.ascontiguousarray(
        np.broadcast_to(np.asarray(L0_ffn_w3_s, np.float32)[None, :], (128, D)))
    import ml_dtypes
    ident = np.eye(128, dtype=ml_dtypes.bfloat16)

    nc = build_program(cfg)
    in_maps = []
    for c in range(NC):
        m = {"x": np.ascontiguousarray(x[c * T:(c + 1) * T]),
             "scl": scl, "s3b": s3b, "ident": ident}
        m.update(packs)
        in_maps.append(m)
    res = run_bass_kernel_spmd(nc, in_maps, core_ids=list(range(NC)))
    out = np.concatenate([r["out"] for r in res.results], axis=0)
    return out.reshape(1, S, D)


# version 23
# speedup vs baseline: 1.1179x; 1.1179x over previous
import numpy as np
from contextlib import ExitStack

import concourse.bass as bass
import concourse.mybir as mybir
from concourse.bass_utils import run_bass_kernel_spmd

F32 = mybir.dt.float32
BF16 = mybir.dt.bfloat16
AF = mybir.ActivationFunctionType
ALU = mybir.AluOpType

ENGINES = ("sync", "scalar", "vector", "tensor", "gpsimd")


def real_cfg():
    return dict(n_cores=8, D=1024, HID=4096, H=16, DK=64, T=2048, TC=1024,
                slab_budget=4096)


def slab_geom(O, I, budget):
    KCH = I // 128
    G = min(O, max(128, (budget // KCH) // 128 * 128))
    while O % G:
        G -= 128
    return G, O // G, KCH * G


def build_program(cfg):
    D, HID, H, DK = cfg["D"], cfg["HID"], cfg["H"], cfg["DK"]
    T, TC = cfg["T"], cfg["TC"]
    CH = T // TC
    TT = TC // 128
    KD = D // 128
    KH = HID // 128
    KF = (H * DK) // 128
    NTOK = min(512, TC)
    NH = TC // NTOK
    WN = min(512, D)
    NW = D // WN
    SBGT = cfg["slab_budget"]
    HDK = H * DK
    assert HDK == D and KF == KD and H == 2 * KF

    wg = {}
    for name, O, I in [("wq", D, D), ("wk", D, D), ("wv", D, D), ("wo", D, D),
                       ("w1", HID, D), ("w2", HID, D),
                       ("h1", HID, HID), ("h2", HID, HID)]:
        G, NS, SF = slab_geom(O, I, SBGT)
        wg[name] = dict(O=O, I=I, G=G, NS=NS, SF=SF, MPS=G // 128, KS=I // 128)
    G3_ = min(256, D)
    KS3 = max(1, (KH * G3_ + SBGT - 1) // SBGT)
    while KH % KS3:
        KS3 += 1
    wg["w3"] = dict(O=D, I=HID, G=G3_, NS=(D // G3_) * KS3, SF=(KH // KS3) * G3_,
                    MPS=G3_ // 128, KS=KH // KS3)
    SLOT_F = max(w["SF"] for w in wg.values())
    NSLOT = 6

    nc = bass.Bass("TRN2", target_bir_lowering=False)

    x_d = nc.dram_tensor("x", [T, D], F32, kind="ExternalInput").ap()
    out_d = nc.dram_tensor("out", [T, D], F32, kind="ExternalOutput").ap()
    w_d = {k: nc.dram_tensor("w_" + k, [wg[k]["NS"], 128, wg[k]["SF"]], F32,
                             kind="ExternalInput").ap() for k in wg}
    scl_d = nc.dram_tensor("scl", [128, 4 * KH], F32, kind="ExternalInput").ap()
    s3b_d = nc.dram_tensor("s3b", [128, D], F32, kind="ExternalInput").ap()
    ident_d = nc.dram_tensor("ident", [128, 128], BF16, kind="ExternalInput").ap()
    cc_in = nc.dram_tensor("cc_in", [64, HDK], F32)
    cc_out = nc.dram_tensor("cc_out", [64, HDK], F32)

    est = ExitStack()
    sbt = lambda n, s, d: est.enter_context(nc.sbuf_tensor(n, s, d))
    pst = lambda n, s, d: est.enter_context(nc.psum_tensor(n, s, d))

    xnf = sbt("xnf", [128, KD * TC], BF16)
    ring = sbt("ring", [128, NSLOT * SLOT_F], BF16)
    hA = sbt("hA", [128, KH * TC], BF16)
    hB = sbt("hB", [128, KH * TC], BF16)
    scratch = sbt("scratch", [128, max(TC, 2 * WN)], F32)
    s3b = sbt("sb_s3b", [128, D], F32)
    scl = sbt("sb_scl", [128, 4 * KH], F32)
    kvacc = sbt("kvacc", [64, HDK], F32)
    kvbf = sbt("kvbf", [128, HDK], BF16)
    ident = sbt("sb_ident", [128, 128], BF16)
    NRAW = CH * TT * 2
    RSTD0, NMR0, ST60 = NRAW, NRAW + CH * TT, NRAW + 2 * CH * TT
    stats = sbt("stats", [128, ST60 + 24], F32)
    hAf = hA.bitcast(F32)
    hBf = hB.bitcast(F32)

    XNT1, KT0, VT0 = 0, TT * D, 2 * TT * D
    XNT2 = 3 * TT * D
    QF0, AF0 = 2 * TT * D, 3 * TT * D
    assert KH * TC >= 4 * TT * D

    psmm = [pst("psA", [128, 512], F32), pst("psB", [128, 512], F32),
            pst("psC", [128, 512], F32), pst("psD", [128, 512], F32)]
    pskv = pst("pskv", [128, 1024], F32)
    pstr = pst("pstr", [128, 2048], BF16)
    pstrf = pstr.bitcast(F32)
    PS8 = [psmm[0][:, :], psmm[1][:, :], psmm[2][:, :], psmm[3][:, :],
           pskv[:, 0:512], pskv[:, 512:1024], pstrf[:, 0:512], pstrf[:, 512:1024]]

    sems = {n: est.enter_context(nc.semaphore(n))
            for n in (("pe", "act", "dve", "po", "cc", "qc", "qd", "qi", "qx", "qo")
                      + tuple("qg%d" % i for i in range(NSLOT)))}
    cnt = {n: 0 for n in sems}
    steps = {e: [] for e in ENGINES}
    hwm = {}

    def wait(eng, s, v):
        if v is None or v <= 0 or hwm.get((eng, s), 0) >= v:
            return
        hwm[(eng, s)] = v
        import sys as _sys
        steps[eng].append(("w", s, v, _sys._getframe(1).f_lineno))

    def emit(eng, fn, inc=None, amt=1):
        tok = None
        if inc is not None:
            cnt[inc] += amt
            tok = cnt[inc]
        import sys as _sys
        steps[eng].append(("i", fn, inc, amt, _sys._getframe(1).f_lineno))
        return tok

    def dma(eng, qsem, dst, src):
        if qsem not in ("qc",):
            wait(eng, qsem, cnt[qsem])
        return emit(eng, lambda e, o=dst, i=src: e.dma_start(out=o, in_=i), qsem, 16)

    def mm(pp, lhs, rhs, start, stop, inc=False):
        return emit("tensor",
                    lambda e, o=pp, l=lhs, r=rhs, st=start, sp=stop:
                    e.matmul(o, l, r, start=st, stop=sp),
                    "pe" if (inc and stop) else None, 1)

    def tr(pp, src):
        return emit("tensor", lambda e, o=pp, i=src: e.transpose(o, i, ident[:, :]),
                    "pe", 1)

    def act(out, in_, func, scale=None, bias=None):
        def f(e, o=out, i=in_, fu=func, sc=scale, bi=bias):
            kw = {}
            if sc is not None:
                kw["scale"] = sc
            if bi is not None:
                kw["bias"] = bi
            return e.activation(o, i, fu, **kw)
        return emit("scalar", f, "act", 1)

    t_ident_qi = dma("sync", "qd", ident[:, :], ident_d[:, :])

    def load_constants():
        dma("sync", "qc", scl[:, :], scl_d[:, :])
        dma("sync", "qc", s3b[:, :], s3b_d[:, :])
        return cnt["qc"]
    t_scl = t_s3b = 32

    slot_free = [0] * NSLOT
    slab_gi = [0]

    def load_slab(name, s):
        slot = slab_gi[0] % NSLOT
        slab_gi[0] += 1
        wait("gpsimd", "pe", slot_free[slot])
        tok_ = dma("gpsimd", "qg%d" % slot,
                   ring[:, slot * SLOT_F: slot * SLOT_F + wg[name]["SF"]],
                   w_d[name][s])
        return ("qg%d" % slot, tok_), slot

    def stage_sync():
        wait("tensor", "act", cnt["act"])
        wait("tensor", "dve", cnt["dve"])
        wait("tensor", "po", cnt["po"])

    def wtile(name, slot, k, col, width):
        base = slot * SLOT_F + k * wg[name]["G"] + col
        return ring[:, base:base + width]

    tok = dict(pe_kv=0, dve_kvevac=0, act_norm={}, dve_bn={}, pe_tr={},
               pe_h1={}, pe_h2={}, pe_w3={}, pe_wo={}, dve_comb={},
               pe_attn={}, act_q={}, qx_x0h=0,
               act_kvready=0, act_trevac=0, qs_kvdup=0, qo_out=0)

    XNT = lambda ch: (0 if ch % 2 == 0 else 3 * TT * D)
    KT0, VT0 = TT * D, 2 * TT * D
    QF0 = 2 * TT * D
    AFH = 2 * TT * D
    XB = lambda ch: (ch % 2) * TT * D

    norm_toks = {}

    def stats_and_norms(ch):
        wait("sync", "act", tok["act_norm"].get(ch - 2, 0))
        wait("sync", "dve", tok["dve_bn"].get(ch - 2, 0))
        NXS = max(1, TT // 2) if ch else max(1, TT // 1)
        t_xs = {}
        for xs in range(NXS):
            r0 = ch * TC + xs * (TC // NXS)
            srcx = x_d[r0:r0 + TC // NXS, :].rearrange("(t p) f -> p t f", p=128)
            o0 = XB(ch) + xs * (TT // NXS) * D
            dstx = hAf[:, o0:o0 + (TT // NXS) * D].rearrange("p (t f) -> p t f", f=D)
            t_xs[xs] = dma("sync", "qx", dstx, srcx)
        if ch == 0:
            tok["qx_x0h"] = t_xs[max(0, NXS // 2 - 1)]
            load_constants()

        cols = lambda t: (stats[:, 2 * (ch * TT + t):2 * (ch * TT + t) + 1],
                          stats[:, 2 * (ch * TT + t) + 1:2 * (ch * TT + t) + 2],
                          stats[:, RSTD0 + ch * TT + t:RSTD0 + ch * TT + t + 1],
                          stats[:, NMR0 + ch * TT + t:NMR0 + ch * TT + t + 1])
        eps_tok = {}
        for t in range(TT):
            wait("vector", "qx", t_xs[t // (TT // NXS)])
            xt = hAf[:, XB(ch) + t * D:XB(ch) + (t + 1) * D]
            st6 = stats[:, ST60 + (t % 2) * 12:ST60 + (t % 2) * 12 + 6]
            st6b = stats[:, ST60 + (t % 2) * 12 + 6:ST60 + (t % 2) * 12 + 12]
            emit("vector", lambda e, o=st6, i=xt[:, 0:D // 2]: e.bn_stats(o, i),
                 "dve", 1)
            b1 = emit("vector", lambda e, o=st6b, i=xt[:, D // 2:D]: e.bn_stats(o, i),
                      "dve", 1)
            wait("vector", "dve", b1)
            mean_c, var_c, rstd_c, nmr_c = cols(t)
            ag = emit("vector", lambda e, o=stats[:, 2 * (ch * TT + t):2 * (ch * TT + t) + 2],
                      i=stats[:, ST60 + (t % 2) * 12:ST60 + (t % 2) * 12 + 12]:
                      e.bn_aggr(o, i), "dve", 1)
            wait("vector", "dve", ag)
            eps_tok[t] = emit("vector", lambda e, o=var_c, i=var_c:
                              e.tensor_scalar_add(o, i, 1e-5), "dve", 1)
        ex_tok = {}
        for t in range(TT):
            mean_c, var_c, rstd_c, nmr_c = cols(t)
            wait("scalar", "dve", eps_tok[t])
            ln = act(var_c, var_c, AF.Ln)
            wait("scalar", "act", ln)
            ex_tok[t] = act(rstd_c, var_c, AF.Exp, scale=-0.5)
        nm_tok = {}
        for t in range(TT):
            mean_c, var_c, rstd_c, nmr_c = cols(t)
            wait("vector", "act", ex_tok[t])
            nm = emit("vector", lambda e, o=nmr_c, a=mean_c, b=rstd_c:
                      e.tensor_mul(o, a, b), "dve", 1)
            wait("vector", "dve", nm)
            nm_tok[t] = emit("vector", lambda e, o=nmr_c, i=nmr_c:
                             e.tensor_scalar_mul(o, i, -1.0), "dve", 1)
        tok["dve_bn"][ch] = nm_tok[TT - 1]
        for t in range(TT):
            mean_c, var_c, rstd_c, nmr_c = cols(t)
            wait("scalar", "dve", nm_tok[t])
            wait("scalar", "pe", tok["pe_kv"])
            norm_toks[(ch, t)] = act(
                hB[:, XNT(ch) + t * D:XNT(ch) + (t + 1) * D],
                hAf[:, XB(ch) + t * D:XB(ch) + (t + 1) * D], AF.Identity,
                scale=rstd_c, bias=nmr_c)
        tok["act_norm"][ch] = norm_toks[(ch, TT - 1)]

    def do_transposes(ch, src_off, src_buf=None, ntoks=None):
        buf = hB if src_buf is None else src_buf
        wait("tensor", "dve", cnt["dve"])
        evac = {}
        xnf_r = xnf.rearrange("p (j tc) -> p j tc", tc=TC)
        last_tr = None
        for t in range(TT):
            half = t % 2
            nt = norm_toks.get((ch, t), 0) if ntoks is None else ntoks.get(t, 0)
            wait("tensor", "act", nt or 0)
            wait("tensor", "qd", t_ident_qi)
            wait("tensor", "act", tok["act_trevac"])
            wait("tensor", "act", evac.get(t - 2, 0))
            for j in range(KD):
                last_tr = tr(pstr[:, half * 1024 + j * 128:half * 1024 + (j + 1) * 128],
                             buf[:, src_off + t * D + j * 128:src_off + t * D + (j + 1) * 128])
            wait("scalar", "pe", last_tr)
            src_t = pstr[:, half * 1024:half * 1024 + KD * 128].rearrange(
                "p (j c) -> p j c", c=128)
            evac[t] = act(xnf_r[:, :, t * 128:(t + 1) * 128], src_t, AF.Copy)
        tok["act_trevac"] = evac[TT - 1]
        return last_tr, evac[TT - 1], evac

    p1b0_kv0 = [0]

    def phase1b(ch, last_tr, xnf_ready, tr_evacs, snap=None):
        tok["pe_tr"][ch] = last_tr

        if snap is not None:
            wait("tensor", "act", snap[0])
            wait("tensor", "dve", snap[1])
            wait("tensor", "po", snap[2])
        else:
            stage_sync()
        if ch == 0:
            wait("gpsimd", "qx", tok.get("qx_x0h", 0))
        else:
            wait("gpsimd", "pe", p1b0_kv0[0])
        GW = wg["wk"]["G"]
        kslabs = [load_slab("wk", s) for s in range(wg["wk"]["NS"])]
        vslabs = [load_slab("wv", s) for s in range(wg["wv"]["NS"])]
        kv_evac = {}
        for t in range(TT):
            for wt, _ in kslabs + vslabs:
                wait("tensor", wt[0], wt[1])
            wait("tensor", "act", tr_evacs[t])
            q8 = (t % 2) * 4
            wait("tensor", "dve", kv_evac.get((t - 2, NW - 1, 1), 0))
            for wi, (wname, slabs) in enumerate((("wk", kslabs), ("wv", vslabs))):
                g = None
                for ow in range(NW):
                    s_i = (ow * WN) // GW
                    col = (ow * WN) % GW
                    wsl = slabs[s_i][1]
                    for k in range(KD):
                        g = mm(PS8[q8 + 2 * wi + ow][:, 0:WN],
                               xnf[:, k * TC + t * 128:k * TC + (t + 1) * 128],
                               wtile(wname, wsl, k, col, WN),
                               k == 0, k == KD - 1, inc=(k == KD - 1 and ow == NW - 1))
                wait("vector", "pe", g)
                dst0 = KT0 if wi == 0 else VT0
                for ow in range(NW):
                    kv_evac[(t, ow, wi)] = emit(
                        "vector",
                        lambda e, o=hB[:, dst0 + t * D + ow * WN:
                                       dst0 + t * D + ow * WN + WN],
                        i=PS8[q8 + 2 * wi + ow][:, 0:WN]: e.tensor_copy(o, i), "dve", 1)
            if ch == 0 and t == 0:
                p1b0_kv0[0] = cnt["pe"]
        for _, wsl in kslabs + vslabs:
            slot_free[wsl] = cnt["pe"]
        kv_proj_done_act = cnt["dve"]

        stage_sync()
        last_kv = None
        for h in range(H):
            wait("tensor", "dve", tok["dve_kvevac"])
            wait("tensor", "dve", kv_proj_done_act)
            for t in range(TT):
                last_kv = mm(pskv[0:64, h * DK:(h + 1) * DK],
                             hB[:, KT0 + t * D + h * DK:KT0 + t * D + (h + 1) * DK],
                             hB[:, VT0 + t * D + h * DK:VT0 + t * D + (h + 1) * DK],
                             t == 0, t == TT - 1, inc=(t == TT - 1))
        tok["pe_kv"] = last_kv
        wait("vector", "pe", last_kv)
        if ch == 0:
            tok["dve_kvevac"] = emit("vector", lambda e, o=kvacc[:, :],
                                     i=pskv[0:64, 0:HDK]: e.tensor_copy(o, i),
                                     "dve", 1)
        else:
            tok["dve_kvevac"] = emit("vector", lambda e, o=kvacc[:, :],
                                     a=kvacc[:, :], b=pskv[0:64, 0:HDK]:
                                     e.tensor_add(o, a, b), "dve", 1)


    stats_and_norms(0)
    tr0 = do_transposes(0, XNT(0))
    snap0 = (cnt["act"], cnt["dve"], cnt["po"])
    if CH > 1:
        stats_and_norms(1)
    phase1b(0, *tr0, snap=snap0)
    for ch in range(1, CH):
        if ch + 1 < CH:
            stats_and_norms(ch + 1)
        trc = do_transposes(ch, XNT(ch))
        snapc = (cnt["act"], cnt["dve"], cnt["po"])
        phase1b(ch, *trc, snap=snapc)

    wait("gpsimd", "dve", tok["dve_kvevac"])
    t_ccin = dma("gpsimd", "qi", cc_in.ap(), kvacc[:, :])
    wait("gpsimd", "qi", t_ccin)
    n_cores = cfg["n_cores"]
    emit("gpsimd", lambda e: e.collective_compute(
        "AllReduce", ALU.add, replica_groups=[list(range(n_cores))],
        ins=[cc_in.ap().opt()], outs=[cc_out.ap().opt()]), "cc", 1)
    t_cc = cnt["cc"]

    prep = {}
    for ch in range(CH):
        if ch == 0:
            last_tr, xnf_ready, _ = do_transposes(ch, XNT(ch))
        else:
            last_tr, xnf_ready = prep[ch]
        tok["pe_tr"][ch] = last_tr

        stage_sync()
        pre_pair = tok.pop("pre_w1w2", None) if ch in tok.get("pre_for", {}) else None
        if tok.get("pre_for_ch") == ch:
            pre_pair = tok.pop("pre_w1w2")
        else:
            pre_pair = None
        NSw = wg["w1"]["NS"]
        MPS = wg["w1"]["MPS"]
        a_evac = {}
        h_done = {}
        dve_bn_last = tok["dve_bn"].get(CH - 1, 0)

        def w1w2_compute(s, w1t, w1s, w2t, w2s):
            for mi in range(MPS):
                m = s * MPS + mi
                q8 = (m % 2) * 4
                wait("tensor", w1t[0], w1t[1])
                wait("tensor", "act", xnf_ready)
                wait("tensor", "act", a_evac.get((m - 2, NH - 1), 0))
                wait("tensor", "dve", h_done.get((m - 2, NH - 1), 0))
                g1 = {}
                for nh in range(NH):
                    for k in range(KD):
                        g1[nh] = mm(PS8[q8 + nh][:, 0:NTOK],
                                    wtile("w1", w1s, k, mi * 128, 128),
                                    xnf[:, k * TC + nh * NTOK:k * TC + (nh + 1) * NTOK],
                                    k == 0, k == KD - 1,
                                    inc=(k == KD - 1))
                wait("tensor", w2t[0], w2t[1])
                g2 = {}
                for nh in range(NH):
                    for k in range(KD):
                        g2[nh] = mm(PS8[q8 + 2 + nh][:, 0:NTOK],
                                    wtile("w2", w2s, k, mi * 128, 128),
                                    xnf[:, k * TC + nh * NTOK:k * TC + (nh + 1) * NTOK],
                                    k == 0, k == KD - 1,
                                    inc=(k == KD - 1))
                for nh in range(NH):
                    wait("scalar", "pe", g1[nh])
                    wait("scalar", "qc", t_scl)
                    wait("scalar", "dve", h_done.get((m - 1, nh), 0))
                    a_evac[(m, nh)] = act(scratch[:, nh * NTOK:(nh + 1) * NTOK],
                                          PS8[q8 + nh][:, 0:NTOK], AF.Silu,
                                          scale=scl[:, m:m + 1])
                    wait("vector", "pe", g2[nh])
                    wait("vector", "act", a_evac[(m, nh)])
                    wait("vector", "qc", t_scl)
                    wait("vector", "dve", dve_bn_last)
                    wait("vector", "qo", tok["qo_out"])
                    h_done[(m, nh)] = emit(
                        "vector",
                        lambda e, o=hA[:, m * TC + nh * NTOK:m * TC + (nh + 1) * NTOK],
                        i0=PS8[q8 + 2 + nh][:, 0:NTOK],
                        sc=scl[:, KH + m:KH + m + 1],
                        i1=scratch[:, nh * NTOK:(nh + 1) * NTOK]:
                        e.scalar_tensor_tensor(o, i0, sc, i1, ALU.mult, ALU.mult),
                        "dve", 1)
            return cnt["pe"]

        prev = None
        for s in range(NSw):
            if prev is not None:
                done = w1w2_compute(*prev)
                slot_free[prev[2]] = done
                slot_free[prev[4]] = done
            if s == 0 and pre_pair is not None:
                (w1t, w1s), (w2t, w2s) = pre_pair
            else:
                w1t, w1s = load_slab("w1", s)
                w2t, w2s = load_slab("w2", s)
            prev = (s, w1t, w1s, w2t, w2s)
        done = w1w2_compute(*prev)
        slot_free[prev[2]] = done
        slot_free[prev[4]] = done
        h_all_dve = cnt["dve"]

        def hid_layer(wname, src_buf, dst_buf, scl_off, pe_tok_dict,
                      src_ready_eng, src_ready_tok):
            NSh = wg[wname]["NS"]
            MPSh = wg[wname]["MPS"]
            ev = {}
            prevh = None

            def comp(s, wt, wsl):
                for mi in range(MPSh):
                    m = s * MPSh + mi
                    nq = 8 // NH
                    q8 = (m % nq) * NH
                    wait("tensor", wt[0], wt[1])
                    wait("tensor", src_ready_eng, src_ready_tok)
                    dist = nq
                    wait("tensor", "act", ev.get((m - dist, NH - 1), 0))
                    g = {}
                    for nh in range(NH):
                        for k in range(KH):
                            g[nh] = mm(PS8[q8 + nh][:, 0:NTOK],
                                       wtile(wname, wsl, k, mi * 128, 128),
                                       src_buf[:, k * TC + nh * NTOK:
                                               k * TC + (nh + 1) * NTOK],
                                       k == 0, k == KH - 1, inc=(k == KH - 1))
                    for nh in range(NH):
                        wait("scalar", "pe", g[nh])
                        wait("scalar", "qc", t_scl)
                        ev[(m, nh)] = act(
                            dst_buf[:, m * TC + nh * NTOK:m * TC + (nh + 1) * NTOK],
                            PS8[q8 + nh][:, 0:NTOK], AF.Gelu,
                            scale=scl[:, scl_off + m:scl_off + m + 1])
                return cnt["pe"]

            stage_sync()
            for s in range(NSh):
                if prevh is not None:
                    d = comp(*prevh)
                    slot_free[prevh[2]] = d
                wt, wsl = load_slab(wname, s)
                prevh = (s, wt, wsl)
            d = comp(*prevh)
            slot_free[prevh[2]] = d
            pe_tok_dict[ch] = cnt["pe"]
            return cnt["act"]

        h1_act_done = hid_layer("h1", hA, hB, 2 * KH, tok["pe_h1"],
                                "dve", h_all_dve)
        h2_act_done = hid_layer("h2", hB, hA, 3 * KH, tok["pe_h2"],
                                "act", h1_act_done)

        G3 = wg["w3"]["G"]
        KS3c = KH // wg["w3"]["KS"]
        KSL3 = wg["w3"]["KS"]
        NOG3 = D // G3
        g_ev = {}
        g_pm = {}
        w3_pm_last = [0]

        def w3_comp(og, slabs):
            for t in range(TT):
                gi = og * TT + t
                pp = PS8[gi % 8]
                for wt, _ in slabs:
                    wait("tensor", wt[0], wt[1])
                wait("tensor", "act", h2_act_done)
                wait("tensor", "dve", g_pm.get(gi - 8, 0))
                g = None
                for ks in range(KS3c):
                    wsl = slabs[ks][1]
                    for kl in range(KSL3):
                        k = ks * KSL3 + kl
                        g = mm(pp[:, 0:G3],
                               hA[:, k * TC + t * 128:k * TC + (t + 1) * 128],
                               wtile("w3", wsl, kl, 0, G3),
                               k == 0, k == KH - 1, inc=(k == KH - 1))
                wait("vector", "pe", g)
                wait("vector", "qc", t_s3b)
                wait("vector", "dve", h_all_dve)
                wait("vector", "act", g_ev.get(gi - 2, 0))
                sl = (gi % 2) * G3
                g_pm[gi] = emit("vector",
                                lambda e, o=scratch[:, sl:sl + G3], a=pp[:, 0:G3],
                                b=s3b[:, og * G3:(og + 1) * G3]: e.tensor_mul(o, a, b),
                                "dve", 1)
                w3_pm_last[0] = g_pm[gi]
                wait("scalar", "dve", g_pm[gi])
                wait("scalar", "pe", tok["pe_tr"][ch])
                wait("scalar", "pe", tok["pe_kv"])
                wait("scalar", "act", h2_act_done)
                g_ev[gi] = act(hBf[:, t * D + og * G3:t * D + (og + 1) * G3],
                               scratch[:, sl:sl + G3], AF.Silu)
            return cnt["pe"]

        stage_sync()
        prev3 = None
        for og in range(NOG3):
            if prev3 is not None:
                d = w3_comp(*prev3)
                for _, wsl in prev3[1]:
                    slot_free[wsl] = d
            slabs = [load_slab("w3", og * KS3c + ks) for ks in range(KS3c)]
            prev3 = (og, slabs)
        d = w3_comp(*prev3)
        for _, wsl in prev3[1]:
            slot_free[wsl] = d
        tok["pe_w3"][ch] = cnt["pe"]
        g_act_done = cnt["act"]

        if ch == 0:
            wait("sync", "cc", t_cc)
            t_ccout = dma("sync", "qx", kvacc[:, :], cc_out.ap())
            wait("scalar", "qx", t_ccout)
            tok["act_kvready"] = act(kvbf[0:64, :], kvacc[:, :], AF.Copy,
                                     scale=float(1.0 / np.sqrt(DK)))
            wait("sync", "act", tok["act_kvready"])
            tok["qs_kvdup"] = dma("sync", "qx", kvbf[64:128, :], kvbf[0:64, :])

        wait("sync", "pe", tok["pe_w3"][ch])
        wait("sync", "qo", tok["qo_out"])
        src = x_d[ch * TC:(ch + 1) * TC, :].rearrange("(t p) f -> p t f", p=128)
        dstx = hAf[:, 0:TT * D].rearrange("p (t f) -> p t f", f=D)
        t_xres = dma("sync", "qx", dstx, src)

        NGI3 = NOG3 * TT
        qslabs = [load_slab("wq", s) for s in range(wg["wq"]["NS"])]
        MPSQ = wg["wq"]["MPS"]
        q_ev = {}
        for m in range(KF):
            bank0 = (m % 2) * 2
            wqt, wqs = qslabs[m // MPSQ]
            wait("tensor", wqt[0], wqt[1])
            for nh in range(NH):
                b = bank0 + nh
                if NGI3 > b:
                    gi_last = NGI3 - 1 - ((NGI3 - 1 - b) % 8)
                    wait("tensor", "dve", g_pm.get(gi_last, 0))
                else:
                    wait("tensor", "act", h2_act_done)
            wait("tensor", "dve", q_ev.get((m - 2, NH - 1), 0))
            g = {}
            for nh in range(NH):
                for k in range(KD):
                    g[nh] = mm(psmm[bank0 + nh][:, 0:NTOK],
                               wtile("wq", wqs, k, (m % MPSQ) * 128, 128),
                               xnf[:, k * TC + nh * NTOK:k * TC + (nh + 1) * NTOK],
                               k == 0, k == KD - 1, inc=(k == KD - 1))
            for nh in range(NH):
                wait("vector", "pe", g[nh])
                wait("vector", "pe", tok["pe_kv"])
                wait("vector", "act", h2_act_done)
                wait("vector", "dve", w3_pm_last[0])
                q_ev[(m, nh)] = emit(
                    "vector",
                    lambda e, o=hB[:, QF0 + m * TC + nh * NTOK:
                                   QF0 + m * TC + (nh + 1) * NTOK],
                    i=psmm[bank0 + nh][:, 0:NTOK]: e.tensor_copy(o, i), "dve", 1)
        for _, wsl in qslabs:
            slot_free[wsl] = cnt["pe"]
        q_act_done = cnt["dve"]
        tok["act_q"][ch] = q_act_done

        at_ev = {}
        for kf in range(KF):
            bank0 = (kf % 2) * 2
            wait("tensor", "act", tok["act_kvready"])
            wait("tensor", "qx", tok["qs_kvdup"])
            wait("tensor", "dve", q_ev.get((kf, NH - 1), 0))
            wait("tensor", "dve", q_ev.get((KF - 2 + (kf % 2), NH - 1), 0))
            wait("tensor", "act", at_ev.get((kf - 2, NH - 1), 0))
            for par in range(2):
                h = 2 * kf + par
                for nh in range(NH):
                    mm(psmm[bank0 + nh][par * 64:(par + 1) * 64, 0:NTOK],
                       kvbf[par * 64:par * 64 + 64, h * DK:(h + 1) * DK],
                       hB[par * 64:(par + 1) * 64,
                          QF0 + kf * TC + nh * NTOK:QF0 + kf * TC + (nh + 1) * NTOK],
                       True, True, inc=(par == 1 and nh == NH - 1))
            gl = cnt["pe"]
            for nh in range(NH):
                wait("scalar", "pe", gl)
                wait("scalar", "pe", tok["pe_w3"][ch])
                at_ev[(kf, nh)] = act(hA[:, AFH + kf * TC + nh * NTOK:
                                         AFH + kf * TC + (nh + 1) * NTOK],
                                      psmm[bank0 + nh][:, 0:NTOK], AF.Copy)
        at_act_done = cnt["act"]
        tok["pe_attn"][ch] = cnt["pe"]
        stage_sync()

        if ch + 1 < CH:
            nch = ch + 1
            wait("sync", "pe", tok["pe_attn"][ch])
            wait("sync", "dve", tok["act_q"][ch])
            NXS2 = max(1, TT // 4)
            tx2 = {}
            for xs in range(NXS2):
                r0 = nch * TC + xs * (TC // NXS2)
                srcx = x_d[r0:r0 + TC // NXS2, :].rearrange("(t p) f -> p t f", p=128)
                o0 = TT * D + xs * (TT // NXS2) * D
                dstx2 = hBf[:, o0:o0 + (TT // NXS2) * D].rearrange(
                    "p (t f) -> p t f", f=D)
                tx2[xs] = dma("sync", "qx", dstx2, srcx)
            rn = {}
            for t in range(TT):
                wait("scalar", "qx", tx2[t // (TT // NXS2)])
                wait("scalar", "pe", tok["pe_w3"][ch])
                rn[t] = act(hA[:, 3 * TT * D + t * D:3 * TT * D + (t + 1) * D],
                            hBf[:, TT * D + t * D:TT * D + (t + 1) * D], AF.Identity,
                            scale=stats[:, RSTD0 + nch * TT + t:RSTD0 + nch * TT + t + 1],
                            bias=stats[:, NMR0 + nch * TT + t:NMR0 + nch * TT + t + 1])
            tok["prep_rn"] = rn

        oslabs = [load_slab("wo", s) for s in range(wg["wo"]["NS"])]
        if ch + 1 < CH:
            tok["pre_w1w2"] = (load_slab("w1", 0), load_slab("w2", 0))
            tok["pre_for_ch"] = ch + 1
        GWO = wg["wo"]["G"]
        comb_pm = {}
        comb = {}
        last_comb = w3_pm_last[0]
        for t in range(TT):
            bank0 = (t % 2) * 2
            for wt, _ in oslabs:
                wait("tensor", wt[0], wt[1])
            wait("tensor", "act", at_act_done)
            wait("tensor", "dve", comb_pm.get((t - 2, NW - 1), 0))
            g = {}
            for ow in range(NW):
                s_i = (ow * WN) // GWO
                col = (ow * WN) % GWO
                wos = oslabs[s_i][1]
                for kf in range(KF):
                    g[ow] = mm(psmm[bank0 + ow][:, 0:WN],
                               hA[:, AFH + kf * TC + t * 128:AFH + kf * TC + (t + 1) * 128],
                               wtile("wo", wos, kf, col, WN),
                               kf == 0, kf == KF - 1, inc=(kf == KF - 1))
            for ow in range(NW):
                wait("vector", "pe", g[ow])
                wait("vector", "act", g_act_done)
                wait("vector", "dve", w3_pm_last[0])
                wait("vector", "po", comb.get((t - 1, ow), 0))
                sl = (ow % 2) * WN
                comb_pm[(t, ow)] = emit(
                    "vector",
                    lambda e, o=scratch[:, sl:sl + WN], a=psmm[bank0 + ow][:, 0:WN],
                    b=hBf[:, t * D + ow * WN:t * D + ow * WN + WN]:
                    e.tensor_mul(o, a, b), "dve", 1)
                wait("gpsimd", "dve", comb_pm[(t, ow)])
                wait("gpsimd", "qx", t_xres)
                last_comb = emit(
                    "gpsimd",
                    lambda e, o=hAf[:, t * D + ow * WN:t * D + ow * WN + WN],
                    a=scratch[:, sl:sl + WN],
                    b=hAf[:, t * D + ow * WN:t * D + ow * WN + WN]:
                    e.tensor_add(o, a, b), "po", 1)
                comb[(t, ow)] = last_comb
            wait("sync", "po", last_comb)
            dsto = out_d[ch * TC + t * 128:ch * TC + (t + 1) * 128, :]
            tok["qo_out"] = dma("sync", "qo", dsto, hAf[:, t * D:(t + 1) * D])
            tok.setdefault("qo_tiles", {}).setdefault(ch, {})[t] = tok["qo_out"]
        for _, wsl in oslabs:
            slot_free[wsl] = cnt["pe"]
        tok["pe_wo"][ch] = cnt["pe"]
        tok["dve_comb"][ch] = cnt["po"]

        if ch + 1 < CH:
            p_tr, p_ready, _ = do_transposes(ch + 1, 3 * TT * D, src_buf=hA,
                                             ntoks=tok.pop("prep_rn"))
            prep[ch + 1] = (p_tr, p_ready)

    wait("sync", "qo", cnt["qo"])
    wait("sync", "qx", cnt["qx"])

    with nc.Block() as block:
        def body(engine_name):
            def run(e):
                for st in steps[engine_name]:
                    if st[0] == "w":
                        e.wait_ge(sems[st[1]], st[2])
                    else:
                        ins = st[1](e)
                        if st[2] is not None:
                            ins.then_inc(sems[st[2]], st[3])
            return run
        block.sync(body("sync"))
        block.scalar(body("scalar"))
        block.vector(body("vector"))
        block.tensor(body("tensor"))
        block.gpsimd(body("gpsimd"))

    sink = globals().get("DEBUG_SINK")
    if isinstance(sink, list):
        sink.append({e: list(steps[e]) for e in steps})
    est.close()
    return nc


def _slab_pack(W, budget, kslices=1):
    O, I = W.shape
    if kslices == 1:
        G, NS, SF = slab_geom(O, I, budget)
        return np.ascontiguousarray(
            W.reshape(NS, G, I // 128, 128).transpose(0, 3, 2, 1)).reshape(NS, 128, SF)
    G = min(256, O)
    NOG = O // G
    KCH = I // 128
    KSL = KCH // kslices
    A = W.reshape(NOG, G, kslices, KSL, 128).transpose(0, 2, 4, 3, 1)
    return np.ascontiguousarray(A).reshape(NOG * kslices, 128, KSL * G)


def kernel(L0_attn_x, L0_attn_W_q, L0_attn_W_k, L0_attn_W_v, L0_attn_W_o,
           L0_ffn_w1_w, L0_ffn_w1_s, L0_ffn_w2_w, L0_ffn_w2_s,
           L0_ffn_h1_w, L0_ffn_h1_s, L0_ffn_h2_w, L0_ffn_h2_s,
           L0_ffn_w3_w, L0_ffn_w3_s):
    cfg = real_cfg()
    NC, T, D, HID = cfg["n_cores"], cfg["T"], cfg["D"], cfg["HID"]
    KH = HID // 128
    B = cfg["slab_budget"]
    x = np.ascontiguousarray(np.asarray(L0_attn_x, dtype=np.float32).reshape(-1, D))
    S = x.shape[0]
    assert S == NC * T

    packs = {
        "w_wq": _slab_pack(np.asarray(L0_attn_W_q, np.float32), B),
        "w_wk": _slab_pack(np.asarray(L0_attn_W_k, np.float32), B),
        "w_wv": _slab_pack(np.asarray(L0_attn_W_v, np.float32), B),
        "w_wo": _slab_pack(np.asarray(L0_attn_W_o, np.float32), B),
        "w_w1": _slab_pack(np.asarray(L0_ffn_w1_w, np.float32), B),
        "w_w2": _slab_pack(np.asarray(L0_ffn_w2_w, np.float32), B),
        "w_h1": _slab_pack(np.asarray(L0_ffn_h1_w, np.float32), B),
        "w_h2": _slab_pack(np.asarray(L0_ffn_h2_w, np.float32), B),
        "w_w3": _slab_pack(np.asarray(L0_ffn_w3_w, np.float32), B,
                           kslices=max(1, (HID // 128 * min(256, D) + B - 1) // B)),
    }
    scl = np.empty((128, 4 * KH), np.float32)
    scl[:, 0 * KH:1 * KH] = np.asarray(L0_ffn_w1_s, np.float32).reshape(KH, 128).T
    scl[:, 1 * KH:2 * KH] = np.asarray(L0_ffn_w2_s, np.float32).reshape(KH, 128).T
    scl[:, 2 * KH:3 * KH] = np.asarray(L0_ffn_h1_s, np.float32).reshape(KH, 128).T
    scl[:, 3 * KH:4 * KH] = np.asarray(L0_ffn_h2_s, np.float32).reshape(KH, 128).T
    s3b = np.ascontiguousarray(
        np.broadcast_to(np.asarray(L0_ffn_w3_s, np.float32)[None, :], (128, D)))
    import ml_dtypes
    ident = np.eye(128, dtype=ml_dtypes.bfloat16)

    nc = build_program(cfg)
    in_maps = []
    for c in range(NC):
        m = {"x": np.ascontiguousarray(x[c * T:(c + 1) * T]),
             "scl": scl, "s3b": s3b, "ident": ident}
        m.update(packs)
        in_maps.append(m)
    res = run_bass_kernel_spmd(nc, in_maps, core_ids=list(range(NC)))
    out = np.concatenate([r["out"] for r in res.results], axis=0)
    return out.reshape(1, S, D)
